# version 10
# baseline (speedup 1.0000x reference)
import numpy as np
import ml_dtypes
import concourse.bass as bass
import concourse.mybir as mybir
from concourse.bass_utils import run_bass_kernel_spmd

F32 = mybir.dt.float32
F32R = mybir.dt.float32r
BF16 = mybir.dt.bfloat16
I32 = mybir.dt.int32
AF = mybir.ActivationFunctionType
ALU = mybir.AluOpType

H = 2048
DI = 4096
DXB = 512
DST = 16
RK = 128
FF = 8192
NF = DI + DXB + DXB + DI + RK
T = 512
EPS = 1e-5

ENGINES = ("t", "v", "s", "p", "sp")
DMA_WAIT_ALL = True
NKSPLIT = 4


class Sem:
    def __init__(self, name):
        self.name = name
        self.handle = None


class Buf:
    __slots__ = ("name", "last_w", "readers")

    def __init__(self, name):
        self.name = name
        self.last_w = None
        self.readers = []


class Step:
    __slots__ = ("idx", "eng", "fn", "reads", "writes", "need_inc", "inc_no",
                 "waits", "is_dma", "stream", "dma_idx", "tag", "small")

    def __init__(self, idx, eng, fn, reads, writes, is_dma, stream, tag):
        self.small = False
        self.idx = idx
        self.eng = eng
        self.fn = fn
        self.reads = reads
        self.writes = writes
        self.need_inc = False
        self.inc_no = None
        self.waits = []
        self.is_dma = is_dma
        self.stream = stream
        self.dma_idx = None
        self.tag = tag


class Sched:
    def __init__(self):
        self.steps = []
        self.bufs = {}
        self.stream_counts = {}

    def buf(self, name):
        b = self.bufs.get(name)
        if b is None:
            b = self.bufs[name] = Buf(name)
        return b

    def add(self, eng, fn, reads=(), writes=(), stream=None, tag="",
            small=False):
        is_dma = stream is not None
        st = Step(len(self.steps), eng, fn, [self.buf(r) for r in reads],
                  [self.buf(w) for w in writes], is_dma, stream, tag)
        st.small = small
        if is_dma:
            st.dma_idx = self.stream_counts.get(stream, 0)
            self.stream_counts[stream] = st.dma_idx + 1
        self.steps.append(st)
        return st

    def dma(self, fn, reads=(), writes=(), stream="d", tag=""):
        return self.add("sp", fn, reads, writes, stream=stream, tag=tag)

    def resolve(self):
        def order_key(p):
            return p

        def add_edge(prod, cons):
            if prod is None or prod is cons:
                return
            if (not prod.is_dma) and (not cons.is_dma) \
                    and prod.eng == cons.eng and not prod.small \
                    and not cons.small:
                return
            prod.need_inc = True
            cons.waits.append(prod)

        for st in self.steps:
            for b in st.reads:
                add_edge(b.last_w, st)
            for b in st.writes:
                add_edge(b.last_w, st)
                for r in b.readers:
                    add_edge(r, st)
            for b in st.reads:
                b.readers.append(st)
            for b in st.writes:
                b.last_w = st
                b.readers = []

        self.stream_step_idxs = {}
        for st in self.steps:
            if st.is_dma:
                self.stream_step_idxs.setdefault(st.stream, []).append(st.idx)

        self.eng_inc_totals = {}
        counts = {}
        for st in self.steps:
            if st.is_dma:
                continue
            if st.need_inc:
                n = counts.get(st.eng, 0) + 1
                counts[st.eng] = n
                st.inc_no = n
        self.eng_inc_totals = counts


def _lanes_rearrange(a4096x16):
    out = np.empty((128, 512), np.float32)
    L = np.arange(128)
    for s in range(512):
        out[:, s] = a4096x16[8 * s + L // 16, L % 16]
    return out


def _col_rearrange(v, cols):
    return np.ascontiguousarray(v.reshape(cols, 128).T)


def build_nc(b_loc=2, L=2048, debug=False, dbg_c=0):
    CH_PER_SEQ = L // T
    CH = b_loc * CH_PER_SEQ
    NJ = 32
    nc = bass.Bass("TRN2", target_bir_lowering=False,
                   detect_race_conditions=False)

    d_x = nc.dram_tensor("x", [b_loc, L, H], F32, kind="ExternalInput")
    d_winT = nc.dram_tensor("winT", [H, NF], F32, kind="ExternalInput")
    d_wdtT = nc.dram_tensor("wdtT", [RK, DI], F32, kind="ExternalInput")
    d_woutT = nc.dram_tensor("woutT", [DI, H], F32, kind="ExternalInput")
    d_gT = nc.dram_tensor("gT", [H, FF], I32, kind="ExternalInput")
    d_uT = nc.dram_tensor("uT", [H, FF], I32, kind="ExternalInput")
    d_dT = nc.dram_tensor("dT", [FF, H], I32, kind="ExternalInput")
    d_alog = nc.dram_tensor("alog", [128, 512], F32, kind="ExternalInput")
    d_bdt = nc.dram_tensor("bdt", [128, 32], F32, kind="ExternalInput")
    d_cw = nc.dram_tensor("cw", [128, 128], F32, kind="ExternalInput")
    d_cb = nc.dram_tensor("cb", [128, 32], F32, kind="ExternalInput")
    d_Dp = nc.dram_tensor("Dp", [128, 32], F32, kind="ExternalInput")
    d_gsc = nc.dram_tensor("gsc", [128, 64], F32, kind="ExternalInput")
    d_usc = nc.dram_tensor("usc", [128, 64], F32, kind="ExternalInput")
    d_dsc = nc.dram_tensor("dsc", [128, 16], F32, kind="ExternalInput")
    d_rrep = nc.dram_tensor("rrep", [128, 2048], F32, kind="ExternalInput")
    d_rred = nc.dram_tensor("rred", [128, 128], F32, kind="ExternalInput")
    d_ident = nc.dram_tensor("ident", [128, 128], F32, kind="ExternalInput")
    d_o = nc.dram_tensor("o", [b_loc, L, H], F32, kind="ExternalOutput")
    d_dbg = (nc.dram_tensor("dbg", [24, 128, T], F32, kind="ExternalOutput")
             if debug else None)

    w_in = nc.dram_tensor("wb_in", [H, NF], BF16)
    w_dt = nc.dram_tensor("wb_dt", [RK, DI], BF16)
    w_out = nc.dram_tensor("wb_out", [DI, H], BF16)
    w_g = nc.dram_tensor("wb_g", [H, FF], BF16)
    w_u = nc.dram_tensor("wb_u", [H, FF], BF16)
    w_d = nc.dram_tensor("wb_d", [FF, H], BF16)
    XBW = L + 8
    xb_d = nc.dram_tensor("xb_d", [b_loc, DXB, XBW], BF16)
    B_d = nc.dram_tensor("B_d", [2, DXB, T], BF16)
    C_d = nc.dram_tensor("C_d", [2, DI, T], BF16)

    sch = Sched()
    E = {}

    dma_sems = {k: Sem("sd_" + k) for k in ("w", "b", "b2", "x", "st")}
    eng_sems = {e: Sem("se_" + e) for e in ("t", "v", "s", "p")}

    sb_specs = []

    def sb(name, shape, dtype):
        sb_specs.append((name, shape, dtype))
        return name

    sb("Al", [128, 512], F32)
    sb("RrepR", [128, 2048], F32R)
    sb("Rred16", [128, 128], BF16)
    sb("id16", [128, 128], BF16)
    sb("bdt2", [128, 32], F32)
    sb("cw_c", [128, 128], F32)
    sb("cb_c", [128, 32], F32)
    sb("cbh", [128, 32], F32)
    sb("Dph", [128, 32], F32)
    sb("gsch", [128, 64], F32)
    sb("usc_c", [128, 64], F32)
    sb("dsc_c", [128, 16], F32)
    sb("cst", [128, 128], F32)
    sb("xn_fm", [128, 16 * T], BF16)
    sb("dtl", [128, T], BF16)
    for i in range(3):
        sb(f"zr{i}", [128, T], BF16)
    sb("y3", [128, 32 * T], BF16)
    for i in range(3):
        sb(f"dA{i}", [128, T], F32)
        sb(f"duB{i}", [128, T], BF16)
        sb(f"stt{i}", [128, T], F32)
        sb(f"cst16_{i}", [128, T], BF16)
        sb(f"durs3_{i}", [128, T], BF16)
    for i in range(4):
        sb(f"Crep{i}", [128, T], BF16)
    for i in range(2):
        sb(f"up{i}", [128, T], BF16)
        sb(f"delta{i}", [128, T], F32R)
        sb(f"du{i}", [128, T], F32R)
        sb(f"Brep{i}", [128, T], BF16)
        sb(f"xbr{i}", [128, T + 8], BF16)
        sb(f"durs{i}", [128, T], BF16)
        sb(f"cacc{i}", [128, T], BF16)
        sb(f"cacc2_{i}", [128, T], BF16)
        sb(f"ca_{i}", [128, T], BF16)
        sb(f"cth{i}", [128, T], BF16)
        sb(f"y2_{i}", [128, T], BF16)
        sb(f"q{i}", [128, T], BF16)
        sb(f"tzz{i}", [128, T], BF16)
        sb(f"tzg{i}", [128, T], BF16)
        sb(f"gh{i}", [128, T], BF16)
        sb(f"upp{i}", [128, T], BF16)
        sb(f"t1_{i}", [128, T], BF16)
        sb(f"tz2_{i}", [128, T], BF16)
        sb(f"gp{i}", [128, T], BF16)
        sb(f"s16_{i}", [128, T], BF16)
        sb(f"ev16_{i}", [128, T], BF16)
        sb(f"xtok{i}", [128, H], F32)
    sb("tsp1", [128, T], F32)
    sb("tsp2", [128, T], F32)
    sb("carry", [128, 512], F32)
    sb("xntok", [128, H], BF16)
    sb("h8", [128, 8 * T], BF16)
    sb("f_fm", [128, 16 * T], BF16)
    sb("otok", [128, H], F32)
    for i in range(3):
        sb(f"wbg{i}", [128, H], BF16)
    sb("stat", [128, 16], F32)
    sb("epsc", [128, 4], F32)
    if debug:
        sb("dbgst", [128, T], F32)

    ps_specs = [("big0", [128, T], F32), ("big1", [128, T], F32),
                ("psy", [128, T], F32),
                ("rd0", [128, T], F32), ("rd1", [128, T], F32),
                ("rd2", [128, T], F32),
                ("ru0", [128, T], F32), ("ru1", [128, T], F32)]

    SB = {}
    PS = {}

    def t_(fn, reads=(), writes=(), tag=""):
        return sch.add("t", fn, reads, writes, tag=tag)

    def v_(fn, reads=(), writes=(), tag="", small=False):
        return sch.add("v", fn, reads, writes, tag=tag, small=small)

    def s_(fn, reads=(), writes=(), tag="", small=False):
        return sch.add("s", fn, reads, writes, tag=tag, small=small)

    def p_(fn, reads=(), writes=(), tag=""):
        return sch.add("p", fn, reads, writes, tag=tag)

    def dma_(fn, reads=(), writes=(), stream="w", tag=""):
        return sch.dma(fn, reads, writes, stream=stream, tag=tag)

    def dump(idx, src_fn, reads):
        if not debug:
            return
        def fn_cp(e):
            return e.tensor_copy(SB["dbgst"][:, :], src_fn())
        v_(fn_cp, reads=tuple(reads), writes=("dbgst",), tag=f"dbg{idx}")
        def fn_dm(e, idx=idx):
            return e.dma_start(d_dbg.ap()[idx], SB["dbgst"][:, :])
        dma_(fn_dm, reads=("dbgst",), stream="x", tag=f"dbgd{idx}")

    def ld_const(dst, src_handle, shape):
        def fn(e):
            return e.dma_start(SB[dst][0:shape[0], 0:shape[1]],
                               src_handle.ap())
        return fn

    def fn_ld_al(e):
        return e.dma_start(SB["Al"][:, :], d_alog.ap())
    dma_(fn_ld_al, writes=("Al",), tag="ld_al")
    def fn_al_exp(e):
        return e.activation(SB["Al"][:, :], SB["Al"][:, :], AF.Exp)
    s_(fn_al_exp, reads=("Al",), writes=("Al",), tag="al_exp")
    def fn_al_neg(e):
        return e.tensor_scalar(out=SB["Al"][:, :], in0=SB["Al"][:, :],
                               scalar1=-1.0, scalar2=None, op0=ALU.mult)
    v_(fn_al_neg, reads=("Al",), writes=("Al",), tag="al_neg")

    def simple_ld(dst, handle):
        def fn(e):
            return e.dma_start(SB[dst][:, :], handle.ap())
        return fn

    dma_(simple_ld("bdt2", d_bdt), writes=("bdt2",))
    def fn_b2(e):
        return e.tensor_scalar(out=SB["bdt2"][:, :], in0=SB["bdt2"][:, :],
                               scalar1=2.0, scalar2=None, op0=ALU.mult)
    v_(fn_b2, reads=("bdt2",), writes=("bdt2",), small=True)
    dma_(simple_ld("cw_c", d_cw), writes=("cw_c",))
    dma_(simple_ld("cb_c", d_cb), writes=("cb_c",))
    def fn_cbh(e):
        return e.tensor_scalar(out=SB["cbh"][:, :], in0=SB["cb_c"][:, :],
                               scalar1=0.5, scalar2=None, op0=ALU.mult)
    v_(fn_cbh, reads=("cb_c",), writes=("cbh",), small=True)
    dma_(simple_ld("Dph", d_Dp), writes=("Dph",))
    def fn_dph(e):
        return e.tensor_scalar(out=SB["Dph"][:, :], in0=SB["Dph"][:, :],
                               scalar1=0.5, scalar2=None, op0=ALU.mult)
    v_(fn_dph, reads=("Dph",), writes=("Dph",), small=True)
    dma_(simple_ld("gsch", d_gsc), writes=("gsch",))
    def fn_gsch(e):
        return e.tensor_scalar(out=SB["gsch"][:, :], in0=SB["gsch"][:, :],
                               scalar1=0.5, scalar2=None, op0=ALU.mult)
    v_(fn_gsch, reads=("gsch",), writes=("gsch",), small=True)
    dma_(simple_ld("usc_c", d_usc), writes=("usc_c",))
    dma_(simple_ld("dsc_c", d_dsc), writes=("dsc_c",))
    def fn_ld_rrep(e):
        return e.dma_start(SB["xtok0"][:, :].bitcast(F32), d_rrep.ap())
    dma_(fn_ld_rrep, writes=("xtok0",))
    def fn_rrep(e):
        return e.tensor_copy(SB["RrepR"][:, :], SB["xtok0"][:, :])
    v_(fn_rrep, reads=("xtok0",), writes=("RrepR",))
    dma_(ld_const("cst", d_rred, [128, 128]), writes=("cst",))
    def fn_rred(e):
        return e.tensor_copy(SB["Rred16"][:, :], SB["cst"][0:128, 0:128])
    v_(fn_rred, reads=("cst",), writes=("Rred16",))
    dma_(ld_const("cst", d_ident, [128, 128]), writes=("cst",))
    def fn_id(e):
        return e.tensor_copy(SB["id16"][:, :], SB["cst"][0:128, 0:128])
    v_(fn_id, reads=("cst",), writes=("id16",))

    def fn_zero(e):
        return e.memset(SB["cacc0"][:, 0:16], 0.0)
    v_(fn_zero, writes=("cacc0",), small=True)
    def fn_eps(e):
        return e.memset(SB["epsc"][:, :], EPS)
    v_(fn_eps, writes=("epsc",), small=True)
    for bb in range(b_loc):
        def fn_zp(e, bb=bb):
            dst = bass.AP(xb_d, bb * DXB * XBW, [[4 * XBW, 128], [XBW, 4], [1, 3]])
            src = bass.AP(SB["cacc0"], 0, [[T, 128], [1, 4], [1, 3]])
            return e.dma_start(dst, src, allow_slow_non_contiguous=True)
        dma_(fn_zp, reads=("cacc0",), writes=("xb_dram",), stream="b", tag="zp")

    rp_ctr = [0]

    def repack(dst_dram, src_dram, rows, cols, is_int):
        n_rt = rows // 128
        c0 = 0
        pieces = []
        while c0 < cols:
            w = min(2048, cols - c0)
            pieces.append((c0, w))
            c0 += w
        for rt in range(n_rt):
            for (c0, w) in pieces:
                slot = rp_ctr[0] % 2
                rp_ctr[0] += 1
                xbuf = f"xtok{slot}"
                nbuf = "xntok"
                def fn_ld(e, rt=rt, c0=c0, w=w, xbuf=xbuf):
                    src = bass.AP(src_dram, rt * 128 * cols + c0,
                                  [[cols, 128], [1, w]])
                    dst = SB[xbuf][:, 0:w]
                    if is_int:
                        dst = dst.bitcast(I32)
                    return e.dma_start(dst, src)
                dma_(fn_ld, writes=(xbuf,), tag="rpk_ld")
                def fn_cast(e, w=w, xbuf=xbuf):
                    src = SB[xbuf][:, 0:w]
                    if is_int:
                        src = src.bitcast(I32)
                    return e.activation(SB[nbuf][:, 0:w], src, AF.Copy,
                                        bias=(-128.0 if is_int else 0.0))
                s_(fn_cast, reads=(xbuf,), writes=(nbuf,), tag="rpk_cast")
                def fn_st(e, rt=rt, c0=c0, w=w):
                    dst = bass.AP(dst_dram, rt * 128 * cols + c0,
                                  [[cols, 128], [1, w]])
                    return e.dma_start(dst, SB[nbuf][:, 0:w])
                dma_(fn_st, reads=(nbuf,), writes=(dst_dram.name,), tag="rpk_st")

    repack(w_in, d_winT, H, NF, False)
    repack(w_dt, d_wdtT, RK, DI, False)
    repack(w_out, d_woutT, DI, H, False)
    repack(w_g, d_gT, H, FF, True)
    repack(w_u, d_uT, H, FF, True)
    repack(w_d, d_dT, FF, H, True)

    big_rot = ["big0", "big1"]
    rot_state = {"i": 0}

    def next_bank():
        b = big_rot[rot_state["i"] % 2]
        rot_state["i"] += 1
        return b

    wslot_state = {"i": 0}

    def next_wslot():
        w = f"wbg{wslot_state['i'] % 3}"
        wslot_state["i"] += 1
        return w

    def mm_group(lhs_src_dram, lhs_off_fn, nk, rhs_fn, bank, extra_reads=(),
                 tag="mm"):
        wsl = next_wslot()
        def fn_ld(e):
            return e.dma_start(SB[wsl][:, 0:nk * 128], lhs_off_fn())
        dma_(fn_ld, writes=(wsl,), stream="w", tag=tag + "_w")
        for k0 in range(0, nk, NKSPLIT):
            k1 = min(k0 + NKSPLIT, nk)
            def fn_mm(e, k0=k0, k1=k1):
                last = None
                for k in range(k0, k1):
                    last = e.matmul(PS[bank][:, :],
                                    SB[wsl][:, 128 * k:128 * (k + 1)],
                                    rhs_fn(k), start=(k == 0),
                                    stop=(k == nk - 1))
                return last
            t_(fn_mm, reads=(wsl,) + tuple(extra_reads), writes=(bank,),
               tag=tag)
        return wsl

    def win_lhs_ap(m):
        def f():
            return bass.AP(w_in, 128 * m, [[NF, 128], [128 * NF, 16], [1, 128]])
        return f

    n_stores = [0]

    for c in range(CH):
        bi = c // CH_PER_SEQ
        t0 = (c % CH_PER_SEQ) * T
        cslot = c % 2

        for tt in range(4):
            xs = f"xtok{tt % 2}"
            def fn_ldx(e, bi=bi, t0=t0, tt=tt, xs=xs):
                return e.dma_start(SB[xs][:, :],
                                   d_x.ap()[bi, t0 + 128 * tt:t0 + 128 * (tt + 1), :])
            dma_(fn_ldx, writes=(xs,), stream="x", tag=f"ldx{c}_{tt}")
            b_s, b_m, b_q, b_r = (f"st_s{tt}", f"st_m{tt}", f"st_q{tt}",
                                  f"st_r{tt}")
            def fn_sum(e, xs=xs, tt=tt):
                return e.reduce_sum(SB["stat"][:, tt:tt + 1], SB[xs][:, :],
                                    axis=mybir.AxisListType.X)
            v_(fn_sum, reads=(xs,), writes=(b_s,), tag="ln_sum", small=True)
            def fn_sq(e, xs=xs, tt=tt):
                return e.activation(SB["xntok"][:, :], SB[xs][:, :], AF.Square,
                                    accum_out=SB["stat"][:, 8 + tt:9 + tt])
            s_(fn_sq, reads=(xs,), writes=("xntok", b_q), tag="ln_sq",
               small=True)
            def fn_mean(e, tt=tt):
                return e.tensor_scalar(out=SB["stat"][:, 4 + tt:5 + tt],
                                       in0=SB["stat"][:, tt:tt + 1],
                                       scalar1=1.0 / H, scalar2=None,
                                       op0=ALU.mult)
            v_(fn_mean, reads=(b_s,), writes=(b_m,), tag="ln_mean", small=True)
            def fn_va(e, tt=tt):
                return e.scalar_tensor_tensor(
                    out=SB["stat"][:, 12 + tt:13 + tt],
                    in0=SB["stat"][:, 4 + tt:5 + tt], scalar=-1.0,
                    in1=SB["stat"][:, 4 + tt:5 + tt], op0=ALU.mult, op1=ALU.mult)
            v_(fn_va, reads=(b_m,), writes=(b_r,), tag="ln_va", small=True)
            def fn_vb(e, tt=tt):
                return e.tensor_scalar(out=SB["stat"][:, 8 + tt:9 + tt],
                                       in0=SB["stat"][:, 8 + tt:9 + tt],
                                       scalar1=1.0 / H, scalar2=None,
                                       op0=ALU.mult)
            v_(fn_vb, reads=(b_q,), writes=(b_q,), tag="ln_vb", small=True)
            def fn_vc(e, tt=tt):
                return e.tensor_tensor(SB["stat"][:, 12 + tt:13 + tt],
                                       SB["stat"][:, 12 + tt:13 + tt],
                                       SB["stat"][:, 8 + tt:9 + tt], op=ALU.add)
            v_(fn_vc, reads=(b_r, b_q), writes=(b_r,), tag="ln_vc", small=True)
            def fn_rstd1(e, tt=tt):
                return e.activation(SB["stat"][:, 12 + tt:13 + tt],
                                    SB["stat"][:, 12 + tt:13 + tt], AF.Ln,
                                    bias=SB["epsc"][:, 0:1])
            s_(fn_rstd1, reads=(b_r, "epsc"), writes=(b_r,), tag="ln_rstd1",
               small=True)
            def fn_rstd2(e, tt=tt):
                return e.activation(SB["stat"][:, 12 + tt:13 + tt],
                                    SB["stat"][:, 12 + tt:13 + tt], AF.Exp,
                                    scale=-0.5)
            s_(fn_rstd2, reads=(b_r,), writes=(b_r,), tag="ln_rstd2",
               small=True)
            def fn_norm(e, xs=xs, tt=tt):
                return e.tensor_scalar(out=SB["xntok"][:, :], in0=SB[xs][:, :],
                                       scalar1=SB["stat"][:, 4 + tt:5 + tt],
                                       scalar2=SB["stat"][:, 12 + tt:13 + tt],
                                       op0=ALU.subtract, op1=ALU.mult)
            v_(fn_norm, reads=(xs, b_m, b_r), writes=("xntok",), tag="ln_norm")
            for g4 in range(4):
                trb = next_bank()
                def fn_tr(e, tt=tt, g4=g4, trb=trb):
                    pv = PS[trb][:, 0:256].bitcast(BF16)
                    last = None
                    for i in range(4):
                        hb = 4 * g4 + i
                        last = e.transpose(pv[:, 128 * i:128 * (i + 1)],
                                           SB["xntok"][:, 128 * hb:128 * (hb + 1)],
                                           SB["id16"][:, :])
                    return last
                t_(fn_tr, reads=("xntok", "id16"), writes=(trb,), tag="xn_tr")
                def fn_ev(e, tt=tt, g4=g4, trb=trb):
                    src = PS[trb][:, 0:256].bitcast(BF16)[:, 0:512].rearrange(
                        "p (i q) -> p i q", i=4)
                    base = T * (4 * g4) + 128 * tt
                    dst = bass.AP(SB["xn_fm"], base,
                                  [[16 * T, 128], [T, 4], [1, 128]])
                    return e.tensor_copy(dst, src)
                v_(fn_ev, reads=(trb,), writes=("xn_fm",), tag="xn_ev")

        if c == dbg_c:
            dump(0, lambda: SB["xn_fm"][:, 0:T], ["xn_fm"])
        def rhs_xn(k):
            return SB["xn_fm"][:, T * k:T * (k + 1)]

        def win_tile(m, evac_fns, tag):
            bank = next_bank()
            mm_group(None, win_lhs_ap(m), 16, rhs_xn, bank,
                     extra_reads=("xn_fm",), tag=tag)
            for ef in evac_fns:
                ef(bank)

        def ev_dtl(bank):
            def fn(e):
                return e.copy(SB["dtl"][:, :], PS[bank][:, :])
            s_(fn, reads=(bank,), writes=("dtl",), tag="ev_dtl")
        win_tile(72, [ev_dtl], "win72")

        for i in range(4):
            def ev_xb(bank, i=i):
                sl = f"ev16_{i % 2}"
                def fn(e):
                    return e.copy(SB[sl][:, :], PS[bank][:, :])
                s_(fn, reads=(bank,), writes=(sl,), tag="ev_xb")
                def fn_b(e, i=i, bi=bi, t0=t0):
                    dst = bass.AP(xb_d, bi * DXB * XBW + 128 * i * XBW + 3 + t0,
                                  [[XBW, 128], [1, T]])
                    return e.dma_start(dst, SB[sl][:, :])
                dma_(fn_b, reads=(sl,), writes=("xb_dram",), stream="b", tag="bw_xb")
            win_tile(32 + i, [ev_xb], "win_xb")
        for i in range(4):
            def ev_B(bank, i=i):
                sl = f"ev16_{i % 2}"
                def fn(e):
                    return e.activation(SB[sl][:, :], PS[bank][:, :], AF.Copy,
                                        scale=0.5)
                s_(fn, reads=(bank,), writes=(sl,), tag="ev_B")
                def fn_b(e, i=i, cslot=cslot):
                    dst = bass.AP(B_d, cslot * DXB * T + 128 * i * T,
                                  [[T, 128], [1, T]])
                    return e.dma_start(dst, SB[sl][:, :])
                dma_(fn_b, reads=(sl,), writes=(f"B_dram{cslot}",), stream="b", tag="bw_B")
            win_tile(36 + i, [ev_B], "win_B")
        for i in range(32):
            def ev_C(bank, i=i):
                sl = f"ev16_{i % 2}"
                def fn(e):
                    return e.copy(SB[sl][:, :], PS[bank][:, :])
                s_(fn, reads=(bank,), writes=(sl,), tag="ev_C")
                def fn_b(e, i=i, cslot=cslot):
                    dst = bass.AP(C_d, cslot * DI * T + 128 * i * T,
                                  [[T, 128], [1, T]])
                    return e.dma_start(dst, SB[sl][:, :])
                dma_(fn_b, reads=(sl,), writes=(f"C_dram{cslot}",), stream="b", tag="bw_C")
            win_tile(40 + i, [ev_C], "win_C")

        if c == dbg_c:
            dump(1, lambda: SB["dtl"][:, :], ["dtl"])
        def sched_z(j):
            zsl = f"zr{j % 3}"
            bank = next_bank()
            mm_group(None, win_lhs_ap(j), 16, rhs_xn, bank,
                     extra_reads=("xn_fm",), tag="win_z")
            def fn(e, zsl=zsl):
                return e.copy(SB[zsl][:, :], PS[bank][:, :])
            s_(fn, reads=(bank,), writes=(zsl,), tag="ev_z")

        def ffn_gen():
            for e8 in range(8):
                for mloc in range(8):
                    m = 8 * e8 + mloc
                    bank_g = next_bank()
                    def lhs_g(m=m):
                        return bass.AP(w_g, 128 * m,
                                       [[FF, 128], [128 * FF, 16], [1, 128]])
                    yield lambda m=m, b=bank_g, l=lhs_g: mm_group(
                        None, l, 16, rhs_xn, b, extra_reads=("xn_fm",),
                        tag="ffn_g")
                    def ev_g(e, m=m, b=bank_g):
                        e.activation(SB[f"tzg{m % 2}"][:, :], PS[b][:, :],
                                     AF.Tanh, scale=SB["gsch"][:, m:m + 1])
                        return e.mul(SB[f"gh{m % 2}"][:, :], PS[b][:, :],
                                     SB["gsch"][:, m:m + 1])
                    yield lambda m=m, b=bank_g, f=ev_g: s_(
                        f, reads=(b, "gsch"), writes=(f"tzg{m % 2}", f"gh{m % 2}"),
                        tag="ffn_evg")
                    bank_u = next_bank()
                    def lhs_u(m=m):
                        return bass.AP(w_u, 128 * m,
                                       [[FF, 128], [128 * FF, 16], [1, 128]])
                    yield lambda m=m, b=bank_u, l=lhs_u: mm_group(
                        None, l, 16, rhs_xn, b, extra_reads=("xn_fm",),
                        tag="ffn_u")
                    def ev_u(e, m=m, b=bank_u):
                        return e.mul(SB[f"upp{m % 2}"][:, :], PS[b][:, :],
                                     SB["usc_c"][:, m:m + 1])
                    yield lambda m=m, b=bank_u, f=ev_u: s_(
                        f, reads=(b, "usc_c"), writes=(f"upp{m % 2}",),
                        tag="ffn_evu")
                    def h_t1(e, m=m):
                        return e.scalar_tensor_tensor(
                            out=SB[f"t1_{m % 2}"][:, :], in0=SB[f"tzg{m % 2}"][:, :],
                            scalar=1.0, in1=SB[f"gh{m % 2}"][:, :],
                            op0=ALU.add, op1=ALU.mult)
                    yield lambda m=m, f=h_t1: v_(
                        f, reads=(f"tzg{m % 2}", f"gh{m % 2}"),
                        writes=(f"t1_{m % 2}",), tag="ffn_t1")
                    def h_mul(e, m=m, mloc=mloc):
                        return e.tensor_tensor(
                            SB["h8"][:, T * mloc:T * (mloc + 1)],
                            SB[f"t1_{m % 2}"][:, :], SB[f"upp{m % 2}"][:, :],
                            op=ALU.mult)
                    yield lambda m=m, mloc=mloc, f=h_mul: v_(
                        f, reads=(f"t1_{m % 2}", f"upp{m % 2}"), writes=("h8",),
                        tag="ffn_h")
                for mo in range(16):
                    bank_d = next_bank()
                    wsl = next_wslot()
                    def fn_ldd(e, e8=e8, mo=mo, wsl=wsl):
                        src = bass.AP(w_d, 1024 * e8 * H + 128 * mo,
                                      [[H, 128], [128 * H, 8], [1, 128]])
                        return e.dma_start(SB[wsl][:, 0:1024], src)
                    yield lambda f=fn_ldd, wsl=wsl: dma_(
                        f, writes=(wsl,), stream="w", tag="ffn_dw")
                    def mk_mmd(k0, k1):
                        def fn_mmd(e, b=bank_d, wsl=wsl, k0=k0, k1=k1):
                            last = None
                            for k in range(k0, k1):
                                last = e.matmul(
                                    PS[b][:, :],
                                    SB[wsl][:, 128 * k:128 * (k + 1)],
                                    SB["h8"][:, T * k:T * (k + 1)],
                                    start=(k == 0), stop=(k == 7))
                            return last
                        return fn_mmd
                    yield lambda f=mk_mmd(0, 4), b=bank_d, wsl=wsl: t_(
                        f, reads=(wsl, "h8"), writes=(b,), tag="ffn_dmm")
                    yield lambda f=mk_mmd(4, 8), b=bank_d, wsl=wsl: t_(
                        f, reads=(wsl, "h8"), writes=(b,), tag="ffn_dmm")
                    def fn_acc(e, e8=e8, mo=mo, b=bank_d):
                        dst = SB["f_fm"][:, T * mo:T * (mo + 1)]
                        if e8 == 0:
                            return e.tensor_scalar(
                                out=dst, in0=PS[b][:, :],
                                scalar1=SB["dsc_c"][:, mo:mo + 1], scalar2=None,
                                op0=ALU.mult)
                        return e.scalar_tensor_tensor(
                            out=dst, in0=PS[b][:, :],
                            scalar=SB["dsc_c"][:, mo:mo + 1], in1=dst,
                            op0=ALU.mult, op1=ALU.add)
                    yield lambda f=fn_acc, b=bank_d: v_(
                        f, reads=(b, "dsc_c", "f_fm"), writes=("f_fm",),
                        tag="ffn_facc")

        feeder = ffn_gen()
        feeder_done = [False]

        def pull_ffn(k):
            for _ in range(k):
                if feeder_done[0]:
                    return
                try:
                    next(feeder)()
                except StopIteration:
                    feeder_done[0] = True
                    return

        seq_start = (t0 == 0)

        if c == dbg_c:
            dump(16, lambda: SB["carry"][:, :], [f"carry{i}" for i in range(512)])
        NQ = 16 * NJ
        dtb = {}

        def pre_dtw(j):
            wsl = next_wslot()
            def fn_lddt(e, j=j, wsl=wsl):
                return e.dma_start(SB[wsl][:, 0:128],
                                   bass.AP(w_dt, 128 * j, [[DI, 128], [1, 128]]))
            dma_(fn_lddt, writes=(wsl,), stream="w", tag="dt_w")
            dtb[("w", j)] = wsl

        def pre_dtmm(j):
            bank = next_bank()
            wsl = dtb.pop(("w", j))
            def fn_dtmm(e, b=bank, wsl=wsl):
                return e.matmul(PS[b][:, :], SB[wsl][:, 0:128], SB["dtl"][:, :],
                                start=True, stop=True)
            t_(fn_dtmm, reads=(wsl, "dtl"), writes=(bank,), tag="dt_mm")
            dtb[("b", j)] = bank

        def pre_sp(j):
            bank = dtb.pop(("b", j))
            def fn_sp(e, j=j, b=bank):
                e.activation(SB["tsp1"][:, :], PS[b][:, :], AF.Abs,
                             bias=SB["bdt2"][:, j:j + 1])
                e.activation(SB["tsp2"][:, :], SB["tsp1"][:, :], AF.Exp,
                             scale=-1.0)
                e.activation(SB["tsp1"][:, :], SB["tsp2"][:, :], AF.Ln,
                             bias=1.0)
                return e.activation(SB["tsp2"][:, :], PS[b][:, :], AF.Relu,
                                    bias=SB["bdt2"][:, j:j + 1])
            s_(fn_sp, reads=(bank, "bdt2"), writes=("tsp1", "tsp2"),
               tag="softplus")

        def pre_delta(j):
            js = j % 2
            def fn_delta(e, js=js):
                return e.tensor_tensor(SB[f"delta{js}"][:, :], SB["tsp1"][:, :],
                                       SB["tsp2"][:, :], op=ALU.add)
            v_(fn_delta, reads=("tsp1", "tsp2"), writes=(f"delta{js}",),
               tag="delta")

        def pre_xbr(j):
            js = j % 2
            def fn_xbr(e, j=j, js=js, bi=bi, t0=t0):
                src = bass.AP(xb_d, bi * DXB * XBW + 16 * j * XBW + t0,
                              [[0, 8], [XBW, 16], [1, T + 3]])
                return e.dma_start(SB[f"xbr{js}"][:, 0:T + 3], src)
            dma_(fn_xbr, reads=("xb_dram",), writes=(f"xbr{js}",), stream="b",
                 tag="br_xb")

        def pre_conv(j):
            js = j % 2
            def fn_conv(e, j=j, js=js):
                e.tensor_scalar(out=SB[f"cacc{js}"][:, :],
                                in0=SB[f"xbr{js}"][:, 0:T],
                                scalar1=SB["cw_c"][:, 4 * j:4 * j + 1],
                                scalar2=None, op0=ALU.mult)
                e.scalar_tensor_tensor(out=SB[f"cacc2_{js}"][:, :],
                                       in0=SB[f"xbr{js}"][:, 1:T + 1],
                                       scalar=SB["cw_c"][:, 4 * j + 1:4 * j + 2],
                                       in1=SB[f"cacc{js}"][:, :],
                                       op0=ALU.mult, op1=ALU.add)
                e.scalar_tensor_tensor(out=SB[f"cacc{js}"][:, :],
                                       in0=SB[f"xbr{js}"][:, 2:T + 2],
                                       scalar=SB["cw_c"][:, 4 * j + 2:4 * j + 3],
                                       in1=SB[f"cacc2_{js}"][:, :],
                                       op0=ALU.mult, op1=ALU.add)
                return e.scalar_tensor_tensor(
                    out=SB[f"cacc2_{js}"][:, :], in0=SB[f"xbr{js}"][:, 3:T + 3],
                    scalar=SB["cw_c"][:, 4 * j + 3:4 * j + 4],
                    in1=SB[f"cacc{js}"][:, :], op0=ALU.mult, op1=ALU.add)
            v_(fn_conv, reads=(f"xbr{js}", "cw_c"),
               writes=(f"cacc{js}", f"cacc2_{js}"), tag="conv")

        def pre_ca(j):
            js = j % 2
            def fn_ca(e, j=j, js=js):
                e.add(SB[f"ca_{js}"][:, :], SB[f"cacc2_{js}"][:, :],
                      SB["cb_c"][:, j:j + 1])
                return e.activation(SB[f"cth{js}"][:, :], SB[f"cacc2_{js}"][:, :],
                                    AF.Tanh, scale=0.5,
                                    bias=SB["cbh"][:, j:j + 1])
            s_(fn_ca, reads=(f"cacc2_{js}", "cb_c", "cbh"),
               writes=(f"ca_{js}", f"cth{js}"), tag="conv_act")

        def pre_up(j):
            js = j % 2
            def fn_up(e, js=js):
                return e.scalar_tensor_tensor(
                    out=SB[f"up{js}"][:, :], in0=SB[f"cth{js}"][:, :], scalar=1.0,
                    in1=SB[f"ca_{js}"][:, :], op0=ALU.add, op1=ALU.mult)
            v_(fn_up, reads=(f"cth{js}", f"ca_{js}"), writes=(f"up{js}",),
               tag="conv_up")

        def pre_du(j):
            js = j % 2
            def fn_du(e, js=js):
                return e.tensor_tensor(SB[f"du{js}"][:, :], SB[f"delta{js}"][:, :],
                                       SB[f"up{js}"][:, :], op=ALU.mult)
            v_(fn_du, reads=(f"delta{js}", f"up{js}"), writes=(f"du{js}",),
               tag="du")

        def pre_brep(j):
            js = j % 2
            def fn_brep(e, j=j, js=js, cslot=cslot):
                src = bass.AP(B_d, cslot * DXB * T + 16 * j * T,
                              [[0, 8], [T, 16], [1, T]])
                return e.dma_start(SB[f"Brep{js}"][:, :], src)
            dma_(fn_brep, reads=(f"B_dram{cslot}",), writes=(f"Brep{js}",),
                 stream="b", tag="br_B")

        def crep(q):
            crs = f"Crep{(q // 2) % 4}"
            j, m = q // 16, q % 16
            def fn_crep(e, j=j, m=m, cslot=cslot, crs=crs):
                g16 = 128 * j + 8 * m
                src = bass.AP(C_d, cslot * DI * T + g16 * T,
                              [[0, 8], [T, 16], [1, T]])
                return e.dma_start(SB[crs][:, :], src)
            sch.add("p", fn_crep, (f"C_dram{cslot}",), (crs,), stream="b2",
                    tag="br_C")

        def s1(q):
            j, m = q // 16, q % 16
            js, ms = j % 2, q % 2
            rdb, rub = f"rd{q % 3}", f"ru{ms}"
            def fn_rep(e, m=m, js=js, rdb=rdb, rub=rub):
                e.matmul(PS[rdb][:, :], SB["RrepR"][:, 128 * m:128 * (m + 1)],
                         SB[f"delta{js}"][:, :], start=True, stop=True)
                return e.matmul(PS[rub][:, :],
                                SB["RrepR"][:, 128 * m:128 * (m + 1)],
                                SB[f"du{js}"][:, :], start=True, stop=True)
            t_(fn_rep, reads=("RrepR", f"delta{js}", f"du{js}"),
               writes=(rdb, rub), tag="rep")

        def s2(q):
            ms, m3 = q % 2, q % 3
            rdb, rub = f"rd{q % 3}", f"ru{ms}"
            def fn_dA(e, sgl=q, m3=m3, rdb=rdb):
                return e.activation(SB[f"dA{m3}"][:, :], PS[rdb][:, :],
                                    AF.Exp, scale=SB["Al"][:, sgl:sgl + 1])
            s_(fn_dA, reads=(rdb, "Al"), writes=(f"dA{m3}",), tag="dA")
            def fn_dur(e, m3=m3, rub=rub):
                return e.tensor_copy(SB[f"durs3_{m3}"][:, :], PS[rub][:, :])
            v_(fn_dur, reads=(rub,), writes=(f"durs3_{m3}",), tag="dur")

        def s3(q):
            j = q // 16
            js, m3 = j % 2, q % 3
            def fn_duB(e, m3=m3, js=js):
                return e.tensor_tensor(SB[f"duB{m3}"][:, :],
                                       SB[f"durs3_{m3}"][:, :],
                                       SB[f"Brep{js}"][:, :], op=ALU.mult)
            v_(fn_duB, reads=(f"durs3_{m3}", f"Brep{js}"),
               writes=(f"duB{m3}",), tag="duB")
            def fn_scan(e, m3=m3, sgl=q, z0=seq_start):
                init = 0.0 if z0 else SB["carry"][:, sgl:sgl + 1]
                return e.tensor_tensor_scan(
                    SB[f"stt{m3}"][:, :], SB[f"dA{m3}"][:, :],
                    SB[f"duB{m3}"][:, :], init, op0=ALU.mult, op1=ALU.add)
            scan_reads = (f"dA{m3}", f"duB{m3}") + \
                (() if seq_start else (f"carry{q}",))
            v_(fn_scan, reads=scan_reads, writes=(f"stt{m3}",), tag="scan")

        def s6(q):
            m3 = q % 3
            def fn_carry(e, m3=m3, sgl=q):
                return e.tensor_copy(SB["carry"][:, sgl:sgl + 1],
                                     SB[f"stt{m3}"][:, T - 1:T])
            v_(fn_carry, reads=(f"stt{m3}",), writes=(f"carry{q}",),
               tag="cext", small=True)

        def s4(q):
            m3 = q % 3
            crs = f"Crep{(q // 2) % 4}"
            def fn_cst(e, m3=m3, crs=crs):
                return e.tensor_tensor(SB[f"cst16_{m3}"][:, :],
                                       SB[f"stt{m3}"][:, :], SB[crs][:, :],
                                       op=ALU.mult)
            p_(fn_cst, reads=(f"stt{m3}", crs), writes=(f"cst16_{m3}",),
               tag="cstate")

        def s5(q):
            m, m3 = q % 16, q % 3
            def fn_red(e, m=m, m3=m3):
                v4 = m % 4
                grp = m // 4
                return e.matmul(PS["psy"][32 * grp:32 * (grp + 1), :],
                                SB["Rred16"][:, 32 * v4:32 * (v4 + 1)],
                                SB[f"cst16_{m3}"][:, :],
                                start=(v4 == 0), stop=(v4 == 3),
                                tile_position=(0, 32 * grp))
            t_(fn_red, reads=("Rred16", f"cst16_{m3}"), writes=("psy",),
               tag="reduce")

        def epi(j):
            js = j % 2
            def fn_y2(e, j=j, js=js):
                return e.scalar_tensor_tensor(
                    out=SB[f"y2_{js}"][:, :], in0=SB[f"up{js}"][:, :],
                    scalar=SB["Dph"][:, j:j + 1], in1=PS["psy"][:, :],
                    op0=ALU.mult, op1=ALU.add)
            v_(fn_y2, reads=(f"up{js}", "Dph", "psy"), writes=(f"y2_{js}",),
               tag="y2")
            zsl = f"zr{j % 3}"
            def fn_tzz(e, js=js, zsl=zsl):
                return e.activation(SB[f"tzz{js}"][:, :], SB[zsl][:, :],
                                    AF.Tanh, scale=0.5)
            s_(fn_tzz, reads=(zsl,), writes=(f"tzz{js}",), tag="tzz")
            def fn_q(e, js=js, zsl=zsl):
                return e.scalar_tensor_tensor(
                    out=SB[f"q{js}"][:, :], in0=SB[f"tzz{js}"][:, :], scalar=1.0,
                    in1=SB[zsl][:, :], op0=ALU.add, op1=ALU.mult)
            v_(fn_q, reads=(f"tzz{js}", zsl), writes=(f"q{js}",), tag="q")
            def fn_y3(e, j=j, js=js):
                return e.scalar_tensor_tensor(
                    out=SB["y3"][:, T * j:T * (j + 1)], in0=SB[f"y2_{js}"][:, :],
                    scalar=0.5, in1=SB[f"q{js}"][:, :], op0=ALU.mult,
                    op1=ALU.mult)
            v_(fn_y3, reads=(f"y2_{js}", f"q{js}"), writes=("y3",), tag="y3")

        pre_dtw(0)
        pre_dtmm(0); pre_sp(0); pre_delta(0)
        pre_xbr(0); pre_conv(0); pre_ca(0); pre_up(0); pre_du(0); pre_brep(0)
        sched_z(0)
        crep(0); crep(2)
        for q in range(NQ + 5):
            if q < NQ:
                s1(q)
            if 0 <= q - 1 < NQ:
                s2(q - 1)
            if 0 <= q - 2 < NQ:
                s3(q - 2)
            if 0 <= q - 4 < NQ:
                s6(q - 4)
            if 0 <= q - 3 < NQ:
                s4(q - 3)
            if 0 <= q - 4 < NQ:
                s5(q - 4)
                if (q - 4) % 16 == 15:
                    epi((q - 4) // 16)
            if q < NQ:
                mq, jn = q % 16, q // 16 + 1
                if jn < NJ:
                    if mq == 1:
                        pre_brep(jn); pre_xbr(jn)
                    elif mq == 2:
                        pre_dtw(jn)
                    elif mq == 3:
                        pre_dtmm(jn)
                        pre_sp(jn)
                    elif mq == 4:
                        sched_z(jn)
                    elif mq == 6:
                        pre_delta(jn)
                    elif mq == 8:
                        pre_conv(jn)
                    elif mq == 10:
                        pre_ca(jn)
                    elif mq == 12:
                        pre_up(jn)
                    elif mq == 14:
                        pre_du(jn)
            if q + 4 < NQ and (q + 4) % 2 == 0:
                crep(q + 4)
            pull_ffn(3)

        while not feeder_done[0]:
            pull_ffn(8)

        if c == dbg_c:
            dump(10, lambda: SB["f_fm"][:, 0:T], ["f_fm"])
        for mo in range(16):
            bank = next_bank()
            wsl = next_wslot()
            def fn_ldw(e, mo=mo, wsl=wsl):
                src = bass.AP(w_out, 128 * mo,
                              [[H, 128], [128 * H, 16], [1, 128]])
                return e.dma_start(SB[wsl][:, 0:2048], src)
            dma_(fn_ldw, writes=(wsl,), stream="w", tag="wo_w1")
            wsl2 = next_wslot()
            def fn_ldw2(e, mo=mo, wsl2=wsl2):
                src = bass.AP(w_out, 16 * 128 * H + 128 * mo,
                              [[H, 128], [128 * H, 16], [1, 128]])
                return e.dma_start(SB[wsl2][:, 0:2048], src)
            dma_(fn_ldw2, writes=(wsl2,), stream="w", tag="wo_w2")
            for k0 in range(0, 32, 4):
                def fn_mmo(e, b=bank, wsl=wsl, wsl2=wsl2, k0=k0):
                    last = None
                    for k in range(k0, k0 + 4):
                        w = wsl if k < 16 else wsl2
                        kk = k % 16
                        last = e.matmul(PS[b][:, :],
                                        SB[w][:, 128 * kk:128 * (kk + 1)],
                                        SB["y3"][:, T * k:T * (k + 1)],
                                        start=(k == 0), stop=(k == 31))
                    return last
                t_(fn_mmo, reads=(wsl, wsl2, "y3"), writes=(bank,),
                   tag="wo_mm")
            def fn_evs(e, mo=mo, b=bank):
                return e.copy(SB[f"s16_{mo % 2}"][:, :], PS[b][:, :])
            s_(fn_evs, reads=(bank,), writes=(f"s16_{mo % 2}",), tag="ev_s")
            if c == dbg_c and mo == 0:
                dump(11, lambda mo=mo: SB[f"s16_{mo % 2}"][:, :],
                     [f"s16_{mo % 2}"])
            def fn_tz2(e, mo=mo):
                return e.activation(SB[f"tz2_{mo % 2}"][:, :],
                                    SB["f_fm"][:, T * mo:T * (mo + 1)],
                                    AF.Tanh, scale=0.5)
            s_(fn_tz2, reads=("f_fm",), writes=(f"tz2_{mo % 2}",), tag="tz2")
            def fn_gp(e, mo=mo):
                return e.scalar_tensor_tensor(
                    out=SB[f"gp{mo % 2}"][:, :], in0=SB[f"tz2_{mo % 2}"][:, :],
                    scalar=1.0, in1=SB["f_fm"][:, T * mo:T * (mo + 1)],
                    op0=ALU.add, op1=ALU.mult)
            v_(fn_gp, reads=(f"tz2_{mo % 2}", "f_fm"), writes=(f"gp{mo % 2}",),
               tag="gp")
            def fn_t3(e, mo=mo):
                return e.scalar_tensor_tensor(
                    out=SB["f_fm"][:, T * mo:T * (mo + 1)],
                    in0=SB[f"s16_{mo % 2}"][:, :], scalar=0.5,
                    in1=SB[f"gp{mo % 2}"][:, :], op0=ALU.mult, op1=ALU.mult)
            v_(fn_t3, reads=(f"s16_{mo % 2}", f"gp{mo % 2}", "f_fm"),
               writes=("f_fm",), tag="t3")

        for tt in range(4):
            xs = f"xtok{tt % 2}"
            def fn_ldx2(e, bi=bi, t0=t0, tt=tt, xs=xs):
                return e.dma_start(SB[xs][:, :],
                                   d_x.ap()[bi, t0 + 128 * tt:t0 + 128 * (tt + 1), :])
            dma_(fn_ldx2, writes=(xs,), stream="x", tag="ldx2")
            for g4 in range(4):
                trb = next_bank()
                def fn_trt(e, tt=tt, g4=g4, trb=trb):
                    pv = PS[trb][:, 0:256].bitcast(BF16)
                    last = None
                    for i in range(4):
                        mo = 4 * g4 + i
                        last = e.transpose(
                            pv[:, 128 * i:128 * (i + 1)],
                            SB["f_fm"][:, T * mo + 128 * tt:T * mo + 128 * (tt + 1)],
                            SB["id16"][:, :])
                    return last
                t_(fn_trt, reads=("f_fm", "id16"), writes=(trb,), tag="t3_tr")
                def fn_cmb(e, tt=tt, g4=g4, xs=xs, trb=trb):
                    pv = PS[trb][:, 0:256].bitcast(BF16)
                    last = None
                    for i in range(4):
                        mo = 4 * g4 + i
                        last = e.tensor_tensor(
                            SB["otok"][:, 128 * mo:128 * (mo + 1)],
                            SB[xs][:, 128 * mo:128 * (mo + 1)],
                            pv[:, 128 * i:128 * (i + 1)], op=ALU.add)
                    return last
                v_(fn_cmb, reads=(trb, xs, "otok"), writes=("otok",),
                   tag="cmb")
            def fn_sto(e, bi=bi, t0=t0, tt=tt):
                return e.dma_start(
                    d_o.ap()[bi, t0 + 128 * tt:t0 + 128 * (tt + 1), :],
                    SB["otok"][:, :])
            sch.dma(fn_sto, reads=("otok",), writes=(), stream="st",
                    tag="store")
            n_stores[0] += 1

    sch.resolve()

    from contextlib import ExitStack
    with ExitStack() as stack:
        for (name, shape, dtype) in sb_specs:
            SB[name] = stack.enter_context(nc.sbuf_tensor(name, shape, dtype))
        for (name, shape, dtype) in ps_specs:
            PS[name] = stack.enter_context(nc.psum_tensor(name, shape, dtype))
        for k in dma_sems:
            dma_sems[k].handle = stack.enter_context(nc.semaphore("sd_" + k))
        for e in ("t", "v", "s", "p"):
            eng_sems[e].handle = stack.enter_context(nc.semaphore("se_" + e))
        block = stack.enter_context(nc.Block())

        def emit_for(eng_name, eng):
            hw = {}

            import bisect

            def wait_val(prod, cons_idx):
                if prod.is_dma:
                    sem = dma_sems[prod.stream]
                    idxs = sch.stream_step_idxs[prod.stream]
                    if DMA_WAIT_ALL:
                        n = bisect.bisect_left(idxs, cons_idx)
                        n = max(n, prod.dma_idx + 1)
                    else:
                        n = prod.dma_idx + 1
                    return sem, 16 * n
                sem = eng_sems[prod.eng]
                return sem, prod.inc_no

            for st in sch.steps:
                if st.eng != eng_name:
                    continue
                wmap = {}
                for prod in st.waits:
                    sem, val = wait_val(prod, st.idx)
                    if val > wmap.get(sem.name, (None, -1))[1]:
                        wmap[sem.name] = (sem, val)
                for sem, val in wmap.values():
                    if hw.get(sem.name, -1) < val:
                        eng.wait_ge(sem.handle, val)
                        hw[sem.name] = val
                inst = st.fn(eng)
                if st.is_dma:
                    inst.then_inc(dma_sems[st.stream].handle, 16)
                elif st.need_inc:
                    inst.then_inc(eng_sems[st.eng].handle, 1)

        @block.sync
        def _(sync):
            emit_for("sp", sync)
            for k in ("w", "b", "x"):
                sync.wait_ge(dma_sems[k].handle,
                             16 * sch.stream_counts.get(k, 0))
            sync.wait_ge(dma_sems["b2"].handle,
                         16 * sch.stream_counts.get("b2", 0))
            sync.wait_ge(dma_sems["st"].handle, 16 * n_stores[0])

        @block.tensor
        def _(tensor):
            emit_for("t", tensor)

        @block.vector
        def _(vector):
            emit_for("v", vector)

        @block.scalar
        def _(scalar):
            emit_for("s", scalar)

        @block.gpsimd
        def _(gpsimd):
            emit_for("p", gpsimd)

    return nc


def _prep_consts(inputs):
    A_log = np.asarray(inputs["L0_attn_A_log"], np.float32)
    b_dt = np.asarray(inputs["L0_attn_b_dt"], np.float32)
    conv_w = np.asarray(inputs["L0_attn_conv_w"], np.float32)
    conv_b = np.asarray(inputs["L0_attn_conv_b"], np.float32)
    Dp = np.asarray(inputs["L0_attn_D_param"], np.float32)
    gsc = np.asarray(inputs["L0_ffn_gate_scale"], np.float32)
    usc = np.asarray(inputs["L0_ffn_up_scale"], np.float32)
    dsc = np.asarray(inputs["L0_ffn_down_scale"], np.float32)

    alog = _lanes_rearrange(A_log)
    bdt = _col_rearrange(b_dt, 32)
    cw = np.empty((128, 128), np.float32)
    for j in range(32):
        for k in range(4):
            cw[:, 4 * j + k] = conv_w[128 * j + np.arange(128), 0, k]
    cb = _col_rearrange(conv_b, 32)
    Dpc = _col_rearrange(Dp, 32)
    gscc = _col_rearrange(gsc, 64)
    uscc = _col_rearrange(usc, 64)
    dscc = _col_rearrange(dsc, 16)

    rrep = np.zeros((128, 2048), np.float32)
    L = np.arange(128)
    for m in range(16):
        rrep[8 * m + L // 16, 128 * m + L] = 1.0
    rred = np.zeros((128, 128), np.float32)
    for v4 in range(4):
        rred[L, 32 * v4 + 8 * v4 + L // 16] = 1.0
    ident = np.eye(128, dtype=np.float32)
    return dict(alog=alog, bdt=bdt, cw=cw, cb=cb, Dp=Dpc, gsc=gscc, usc=uscc,
                dsc=dscc, rrep=rrep, rred=rred, ident=ident)


def kernel(L0_attn_hidden_states, L0_attn_W_in, L0_attn_W_dt, L0_attn_b_dt,
           L0_attn_conv_w, L0_attn_conv_b, L0_attn_A_log, L0_attn_D_param,
           L0_attn_W_out, L0_ffn_gate_w, L0_ffn_gate_scale, L0_ffn_up_w,
           L0_ffn_up_scale, L0_ffn_down_w, L0_ffn_down_scale):
    inputs = dict(L0_attn_A_log=L0_attn_A_log, L0_attn_b_dt=L0_attn_b_dt,
                  L0_attn_conv_w=L0_attn_conv_w, L0_attn_conv_b=L0_attn_conv_b,
                  L0_attn_D_param=L0_attn_D_param,
                  L0_ffn_gate_scale=L0_ffn_gate_scale,
                  L0_ffn_up_scale=L0_ffn_up_scale,
                  L0_ffn_down_scale=L0_ffn_down_scale)
    consts = _prep_consts(inputs)
    x = np.ascontiguousarray(np.asarray(L0_attn_hidden_states, np.float32))
    winT = np.ascontiguousarray(np.asarray(L0_attn_W_in, np.float32).T)
    wdtT = np.ascontiguousarray(np.asarray(L0_attn_W_dt, np.float32).T)
    woutT = np.ascontiguousarray(np.asarray(L0_attn_W_out, np.float32).T)
    gT = np.ascontiguousarray(np.asarray(L0_ffn_gate_w, np.int32).T)
    uT = np.ascontiguousarray(np.asarray(L0_ffn_up_w, np.int32).T)
    dT = np.ascontiguousarray(np.asarray(L0_ffn_down_w, np.int32).T)

    nc = build_nc(b_loc=2, L=2048)
    in_maps = []
    for cid in range(8):
        m = dict(x=np.ascontiguousarray(x[2 * cid:2 * cid + 2]),
                 winT=winT, wdtT=wdtT, woutT=woutT, gT=gT, uT=uT, dT=dT,
                 **consts)
        in_maps.append(m)
    res = run_bass_kernel_spmd(nc, in_maps, core_ids=list(range(8)))
    out = np.concatenate([r["o"] for r in res.results], axis=0)
    return out.reshape(16, 2048, 2048)
